# version 01
# baseline (speedup 1.0000x reference)
import math
import numpy as np
from contextlib import ExitStack

import concourse.bass as bass
import concourse.mybir as mybir
from concourse.bass_utils import run_bass_kernel_spmd

F32 = mybir.dt.float32
BF16 = mybir.dt.bfloat16
ALU = mybir.AluOpType
ACT = mybir.ActivationFunctionType
AX = mybir.AxisListType.X

P = 128
D = 1024
H = 4096
N = 4096
NCH = 32
DC = 8
HT = 32
NB = 256
NBLK = 16
NSUB = 2
NT = NBLK * NSUB
INV_D = 1.0 / 1024.0
EPS = 1e-5
ISQ = 1.0 / math.sqrt(128.0)
WG1 = 16
WG2 = 16


def build_nc():
    nc = bass.Bass("TRN2", target_bir_lowering=False, detect_race_conditions=False)

    dJ = nc.dram_tensor("J", [N, D], F32, kind="ExternalInput")
    dwqT = nc.dram_tensor("wqT", [D, 128], F32, kind="ExternalInput")
    dwkT = nc.dram_tensor("wkT", [D, 128], F32, kind="ExternalInput")
    dw1T = nc.dram_tensor("w1T", [D, H], F32, kind="ExternalInput")
    dw2T = nc.dram_tensor("w2T", [H, D], F32, kind="ExternalInput")
    dgB = nc.dram_tensor("gB", [P, D], F32, kind="ExternalInput")
    dbB = nc.dram_tensor("bB", [P, D], F32, kind="ExternalInput")
    db2B = nc.dram_tensor("b2B", [P, D], F32, kind="ExternalInput")
    db1P = nc.dram_tensor("b1P", [P, 32], F32, kind="ExternalInput")
    dout = nc.dram_tensor("out", [N, D], F32, kind="ExternalOutput")
    dxnat = nc.dram_tensor("xnat", [N, D], BF16, kind="Internal")
    dattn = nc.dram_tensor("attnscr", [N, D], BF16, kind="Internal")
    da1 = nc.dram_tensor("a1scr", [H], F32, kind="Internal")
    da2 = nc.dram_tensor("a2scr", [D], F32, kind="Internal")

    ctx = ExitStack()
    sb = lambda n, s, d: ctx.enter_context(nc.sbuf_tensor(n, s, d))
    sem = lambda n: ctx.enter_context(nc.semaphore(n))

    with ctx:
        S1T = sb("S1T", [P, DC * H], BF16)
        S2T = sb("S2T", [P, HT * D], BF16)
        alpha1P = sb("alpha1P", [P, 32], F32)
        b1P = sb("b1Ps", [P, 32], F32)
        onesA = sb("onesA", [P, 1], BF16)
        onesH = sb("onesH", [P, 1], BF16)
        onesF = sb("onesF", [1, P], F32)
        junk1 = sb("junk1", [P, 8], F32)
        jctr = [0]
        SU = sb("SU", [P, 32], F32)
        SQ = sb("SQ", [P, 32], F32)
        MU = sb("MU", [P, 32], F32)
        E2E = sb("E2E", [P, 32], F32)
        NEGMU = sb("NEGMU", [P, 32], F32)
        VAR = sb("VAR", [P, 32], F32)
        REC = sb("REC", [P, 32], F32)
        RS = sb("RS", [P, 32], F32)
        NMS = sb("NMS", [P, 32], F32)
        RS2 = sb("RS2", [P, 32], F32)
        NMS2 = sb("NMS2", [P, 32], F32)

        s_j = [sem(f"s_j{k}") for k in range(6)]
        s_xw = [sem(f"s_xw{k}") for k in range(3)]
        s_sq = sem("s_sq")
        s_vrec = sem("s_vrec"); s_rs = sem("s_rs"); s_vnm = sem("s_vnm")
        s_xn = sem("s_xn"); s_w1 = sem("s_w1"); s_sg1 = sem("s_sg1")
        s_ab1 = sem("s_ab1"); s_te1 = sem("s_te1"); s_rb = sem("s_rb")
        s_a1w = [sem(f"s_a1w{k}") for k in range(2)]
        s_a1l = sem("s_a1l")
        s_wqk = sem("s_wqk"); s_cast = sem("s_cast")
        s_trB = [sem(f"s_trB{k}") for k in range(2)]
        s_xc = [sem(f"s_xc{k}") for k in range(2)]
        s_teK = sem("s_teK")
        s_kev = sem("s_kev"); s_teMM = sem("s_teMM")
        s_teQ = sem("s_teQ"); s_qev = sem("s_qev"); s_mev = sem("s_mev")
        s_jr = [sem(f"s_jr{k}") for k in range(2)]
        s_xr = sem("s_xr"); s_vc1 = sem("s_vc1")
        s_sq2 = sem("s_sq2"); s_vrec2 = sem("s_vrec2"); s_rs2 = sem("s_rs2")
        s_vnm2 = sem("s_vnm2"); s_z = sem("s_z"); s_teC = sem("s_teC")
        s_vq = sem("s_vq")
        s_aw = [sem(f"s_aw{k}") for k in range(2)]
        s_w2 = sem("s_w2"); s_sg2 = sem("s_sg2"); s_ab2 = sem("s_ab2")
        s_te2 = sem("s_te2"); s_a2w = sem("s_a2w"); s_a2l = sem("s_a2l")
        s_gbc = sem("s_gbc"); s_cbc = sem("s_cbc"); s_abc = sem("s_abc")
        s_trD = [sem(f"s_trD{k}") for k in range(2)]
        s_mm1 = sem("s_mm1"); s_ge = sem("s_ge")
        s_mm2 = sem("s_mm2"); s_epu = sem("s_epu"); s_epv = sem("s_epv")
        s_vw = sem("s_vw"); s_vo = sem("s_vo")
        s_ow = [sem(f"s_ow{k}") for k in range(2)]
        s_apr = [sem(f"s_apr{k}") for k in range(2)]
        s_jr2 = [sem(f"s_jr2{k}") for k in range(2)]

        def w1_src(t):
            dc, hh = t // 2, t % 2
            return bass.AP(dw1T, dc * 128 * H + hh * 2048, [[H, 128], [1, 2048]])

        w1_slot = {2 + t: t for t in range(WG1)}
        w2_slot = {1 + 2 * t: t for t in range(WG2)}

        sAC = ExitStack()
        QT = sAC.enter_context(nc.sbuf_tensor("QT", [P, N], BF16))
        Msb = sAC.enter_context(nc.sbuf_tensor("Msb", [P, D], BF16))
        wTf = sAC.enter_context(nc.sbuf_tensor("wTf", [P, 2048], F32))
        sgnb = sAC.enter_context(nc.sbuf_tensor("sgnb", [P, 2048], BF16))
        absb = sAC.enter_context(nc.sbuf_tensor("absb", [P, 2048], BF16))
        rowb = sAC.enter_context(nc.sbuf_tensor("rowb", [1, 1024], F32))
        dumpA = sAC.enter_context(nc.sbuf_tensor("dumpA", [P, D], BF16))

        with ExitStack() as sA:
            pal = [sA.enter_context(nc.psum_tensor(f"pal{g}", [1, 512], F32))
                   for g in range(8)]
            Jring = sA.enter_context(nc.sbuf_tensor("Jring", [P, 6 * D], F32))
            XNr = sA.enter_context(nc.sbuf_tensor("XNr", [P, 3 * D], BF16))
            with nc.Block() as blk:
                @blk.sync
                def _(sync):
                    for c in range(4):
                        sync.dma_start(Jring[:, c * D:(c + 1) * D],
                                       dJ[c * 128:(c + 1) * 128, :]).then_inc(s_j[c % 6], 16)
                    sync.dma_start(wTf[:], w1_src(0)).then_inc(s_w1, 16)
                    for c in range(36):
                        if 0 <= c - 4 < NCH:
                            cc = c - 4
                            sync.wait_ge(s_xn, cc + 1)
                            sync.dma_start(dxnat[cc * 128:(cc + 1) * 128, :],
                                           XNr[:, (cc % 3) * D:(cc % 3 + 1) * D]
                                           ).then_inc(s_xw[cc % 3], 16)
                        if c in w1_slot and w1_slot[c] + 1 < WG1:
                            t = w1_slot[c] + 1
                            sync.wait_ge(s_sg1, t)
                            sync.wait_ge(s_ab1, t)
                            sync.dma_start(wTf[:], w1_src(t)).then_inc(s_w1, 16)
                        if 4 <= c + 2 < NCH:
                            if c - 3 >= 1:
                                sync.wait_ge(s_xn, c - 3)
                            sync.dma_start(Jring[:, ((c + 2) % 6) * D:((c + 2) % 6 + 1) * D],
                                           dJ[(c + 2) * 128:(c + 3) * 128, :]
                                           ).then_inc(s_j[(c + 2) % 6], 16)
                    for g in range(8):
                        sync.wait_ge(s_rb, g + 1)
                        sync.dma_start(bass.AP(da1, g * 512, [[1, 1], [1, 512]]),
                                       rowb[:1, (g % 2) * 512:(g % 2) * 512 + 512]
                                       ).then_inc(s_a1w[g % 2], 16)
                    sync.wait_ge(s_a1w[0], 16 * 4)
                    sync.wait_ge(s_a1w[1], 16 * 4)
                    sync.dma_start(alpha1P[:], bass.AP(da1, 0, [[1, 128], [128, 32]]),
                                   allow_slow_non_contiguous=True).then_inc(s_a1l, 16)
                    sync.dma_start(b1P[:], db1P.ap()).then_inc(s_a1l, 16)

                @blk.vector
                def _(vector):
                    vector.memset(onesA[:], INV_D)
                    vector.memset(onesH[:], 1.0 / 4096.0)
                    vector.memset(onesF[:], 1.0)
                    for c in range(36):
                        if c < NCH:
                            vector.wait_ge(s_j[c % 6], 16 * (c // 6 + 1))
                            vector.reduce_sum(SU[:, c:c + 1],
                                              Jring[:, (c % 6) * D:(c % 6 + 1) * D],
                                              axis=AX)
                        if c in w1_slot:
                            t = w1_slot[c]
                            vector.wait_ge(s_w1, 16 * (t + 1))
                            vector.tensor_scalar(sgnb[:], wTf[:], 1e38, 1.0,
                                                 ALU.mult, ALU.min)
                        if 0 <= c - 1 < NCH:
                            vector.wait_ge(s_sq, c)
                            vector.tensor_scalar(E2E[:, c - 1:c], SQ[:, c - 1:c],
                                                 INV_D, EPS, ALU.mult, ALU.add)
                            vector.tensor_scalar(MU[:, c - 1:c], SU[:, c - 1:c],
                                                 INV_D, None, ALU.mult)
                            vector.tensor_scalar(NEGMU[:, c - 1:c], SU[:, c - 1:c],
                                                 -INV_D, None, ALU.mult)
                        if c in w1_slot:
                            t = w1_slot[c]
                            dc, hh = t // 2, t % 2
                            vector.tensor_scalar(
                                S1T[:, dc * H + hh * 2048: dc * H + hh * 2048 + 2048],
                                sgnb[:], -1.0, None, ALU.max).then_inc(s_sg1, 1)
                        if 0 <= c - 2 < NCH:
                            vector.scalar_tensor_tensor(VAR[:, c - 2:c - 1],
                                                        MU[:, c - 2:c - 1],
                                                        NEGMU[:, c - 2:c - 1],
                                                        E2E[:, c - 2:c - 1],
                                                        ALU.mult, ALU.add)
                        if 0 <= c - 3 < NCH:
                            vector.reciprocal(REC[:, c - 3:c - 2],
                                              VAR[:, c - 3:c - 2]).then_inc(s_vrec, 1)
                        if 0 <= c - 4 < NCH:
                            vector.wait_ge(s_rs, c - 3)
                            vector.tensor_tensor(NMS[:, c - 4:c - 3],
                                                 NEGMU[:, c - 4:c - 3],
                                                 RS[:, c - 4:c - 3],
                                                 ALU.mult).then_inc(s_vnm, 1)
                    vector.wait_ge(s_te1, WG1)
                    for g in range(8):
                        if g >= 2:
                            vector.wait_ge(s_a1w[g % 2], 16 * (g // 2))
                        vector.tensor_copy(rowb[:1, (g % 2) * 512:(g % 2) * 512 + 512],
                                           pal[g][:1, :]).then_inc(s_rb, 1)

                @blk.scalar
                def _(scalar):
                    for c in range(36):
                        if 0 <= c - 3 < NCH:
                            scalar.wait_ge(s_vrec, c - 2)
                            scalar.sqrt(RS[:, c - 3:c - 2],
                                        REC[:, c - 3:c - 2]).then_inc(s_rs, 1)
                        if c < NCH:
                            scalar.wait_ge(s_j[c % 6], 16 * (c // 6 + 1))
                            scalar.activation(dumpA[:],
                                              Jring[:, (c % 6) * D:(c % 6 + 1) * D],
                                              ACT.Square,
                                              accum_out=SQ[:, c:c + 1]).then_inc(s_sq, 1)
                        if c in w1_slot:
                            t = w1_slot[c]
                            scalar.wait_ge(s_w1, 16 * (t + 1))
                            if t >= 1:
                                scalar.wait_ge(s_te1, t)
                            scalar.activation(absb[:], wTf[:], ACT.Abs).then_inc(s_ab1, 1)
                        if 0 <= c - 4 < NCH:
                            cc = c - 4
                            scalar.wait_ge(s_vnm, cc + 1)
                            if cc >= 3:
                                scalar.wait_ge(s_xw[cc % 3], 16 * (cc // 3))
                            scalar.activation(XNr[:, (cc % 3) * D:(cc % 3 + 1) * D],
                                              Jring[:, (cc % 6) * D:(cc % 6 + 1) * D],
                                              ACT.Identity, bias=NMS[:, cc:cc + 1],
                                              scale=RS[:, cc:cc + 1]).then_inc(s_xn, 1)

                @blk.tensor
                def _(tensor):
                    for t in range(WG1):
                        dc, hh = t // 2, t % 2
                        tensor.wait_ge(s_ab1, t + 1)
                        for g in range(4):
                            mm = tensor.matmul(pal[4 * hh + g][:1, :], onesA[:],
                                               absb[:, g * 512:(g + 1) * 512],
                                               start=(dc == 0), stop=(dc == 7),
                                               skip_group_check=True)
                        mm.then_inc(s_te1, 1)

        with ExitStack() as sB:
            XT = sB.enter_context(nc.sbuf_tensor("XT", [P, 2 * 2048], BF16))
            XNc = sB.enter_context(nc.sbuf_tensor("XNc", [P, 2 * D], BF16))
            Kb = sB.enter_context(nc.sbuf_tensor("Kb", [P, 2 * 128], BF16))
            wqkf = sB.enter_context(nc.sbuf_tensor("wqkf", [P, 2048], F32))
            WQTb = sB.enter_context(nc.sbuf_tensor("WQTb", [P, D], BF16))
            WKTb = sB.enter_context(nc.sbuf_tensor("WKTb", [P, D], BF16))
            pK = [sB.enter_context(nc.psum_tensor(f"pK{x}", [P, 128], F32)) for x in range(2)]
            pQT = [sB.enter_context(nc.psum_tensor(f"pQT{x}", [P, 256], F32)) for x in range(2)]
            pM = sB.enter_context(nc.psum_tensor("pM", [P, D], F32))
            with nc.Block() as blk:
                @blk.sync
                def _(sync):
                    for nb in range(NBLK):
                        if nb >= 2:
                            sync.wait_ge(s_teQ, nb - 1)
                        for k in range(3):
                            ndone = 2 * nb + 2
                            cnt = (ndone - 1 - k) // 3 + 1 if ndone - 1 >= k else 0
                            if cnt > 0:
                                sync.wait_ge(s_xw[k], 16 * cnt)
                        for dc in range(DC):
                            sync.dma_start(
                                XT[:, (nb % 2) * 2048 + dc * 256:
                                      (nb % 2) * 2048 + dc * 256 + 256],
                                bass.AP(dxnat, nb * 256 * D + dc * 128,
                                        [[D, 256], [1, 128]]),
                                transpose=True).then_inc(s_trB[nb % 2], 16)

                @blk.scalar
                def _(scalar):
                    for dc in range(DC):
                        scalar.dma_start(wqkf[:, dc * 128:(dc + 1) * 128],
                                         dwqT[dc * 128:(dc + 1) * 128, :]).then_inc(s_wqk, 16)
                    for dc in range(DC):
                        scalar.dma_start(wqkf[:, 1024 + dc * 128:1024 + (dc + 1) * 128],
                                         dwkT[dc * 128:(dc + 1) * 128, :]).then_inc(s_wqk, 16)
                    for c in range(NCH):
                        if c >= 2:
                            scalar.wait_ge(s_teMM, c - 1)
                        scalar.dma_start(XNc[:, (c % 2) * D:(c % 2 + 1) * D],
                                         dxnat[c * 128:(c + 1) * 128, :]).then_inc(s_xc[c % 2], 16)

                @blk.vector
                def _(vector):
                    vector.wait_ge(s_wqk, 16 * 16)
                    vector.tensor_copy(WQTb[:], wqkf[:, 0:1024]).then_inc(s_cast, 1)
                    vector.tensor_copy(WKTb[:], wqkf[:, 1024:2048]).then_inc(s_cast, 1)
                    for nb in range(NBLK):
                        for j in range(2):
                            c = 2 * nb + j
                            vector.wait_ge(s_teK, c + 1)
                            vector.tensor_copy(Kb[:, (c % 2) * 128:(c % 2 + 1) * 128],
                                               pK[c % 2][:]).then_inc(s_kev, 1)
                        vector.wait_ge(s_teQ, nb + 1)
                        vector.tensor_copy(QT[:, nb * 256:(nb + 1) * 256],
                                           pQT[nb % 2][:]).then_inc(s_qev, 1)
                    vector.wait_ge(s_teMM, NCH)
                    vector.tensor_copy(Msb[:], pM[:]).then_inc(s_mev, 1)

                @blk.tensor
                def _(tensor):
                    tensor.wait_ge(s_cast, 2)
                    for nb in range(NBLK):
                        xt0 = (nb % 2) * 2048
                        tensor.wait_ge(s_trB[nb % 2], 16 * 8 * (nb // 2 + 1))
                        for j in range(2):
                            c = 2 * nb + j
                            if c >= 2:
                                tensor.wait_ge(s_kev, c - 1)
                            for dc in range(DC):
                                mm = tensor.matmul(
                                    pK[c % 2][:],
                                    XT[:, xt0 + dc * 256 + j * 128:
                                          xt0 + dc * 256 + j * 128 + 128],
                                    WKTb[:, dc * 128:(dc + 1) * 128],
                                    start=(dc == 0), stop=(dc == 7),
                                    skip_group_check=True)
                            mm.then_inc(s_teK, 1)
                        for j in range(2):
                            c = 2 * nb + j
                            tensor.wait_ge(s_kev, c + 1)
                            tensor.wait_ge(s_xc[c % 2], 16 * (c // 2 + 1))
                            for hf in range(2):
                                mm = tensor.matmul(
                                    pM[:, hf * 512:(hf + 1) * 512],
                                    Kb[:, (c % 2) * 128:(c % 2 + 1) * 128],
                                    XNc[:, (c % 2) * D + hf * 512:
                                           (c % 2) * D + hf * 512 + 512],
                                    start=(c == 0), stop=(c == NCH - 1),
                                    skip_group_check=True)
                            mm.then_inc(s_teMM, 1)
                        if nb >= 2:
                            tensor.wait_ge(s_qev, nb - 1)
                        for dc in range(DC):
                            mm = tensor.matmul(
                                pQT[nb % 2][:],
                                WQTb[:, dc * 128:(dc + 1) * 128],
                                XT[:, xt0 + dc * 256: xt0 + dc * 256 + 256],
                                start=(dc == 0), stop=(dc == 7),
                                skip_group_check=True)
                        mm.then_inc(s_teQ, 1)

        with ExitStack() as sC:
            Jr = sC.enter_context(nc.sbuf_tensor("Jr", [P, 2 * D], F32))
            xrb = sC.enter_context(nc.sbuf_tensor("xrb", [P, 2 * D], BF16))
            tbuf = sC.enter_context(nc.sbuf_tensor("tbuf", [P, 6 * D], BF16))
            zbuf = sC.enter_context(nc.sbuf_tensor("zbuf", [P, 2 * D], BF16))
            qbuf = sC.enter_context(nc.sbuf_tensor("qbuf", [P, 2 * D], BF16))
            gBC = sC.enter_context(nc.sbuf_tensor("gBC", [P, D], F32))
            pA = [sC.enter_context(nc.psum_tensor(f"pA{x}", [P, D], F32)) for x in range(2)]
            pal2 = [sC.enter_context(nc.psum_tensor(f"pal2{x}", [1, 512], F32))
                    for x in range(2)]
            with nc.Block() as blk:
                @blk.sync
                def _(sync):
                    sync.dma_start(gBC[:], dgB.ap()).then_inc(s_gbc, 16)
                    for i in range(2):
                        sync.dma_start(Jr[:, (i % 2) * D:(i % 2 + 1) * D],
                                       dJ[i * 128:(i + 1) * 128, :]).then_inc(s_jr[i % 2], 16)
                    for i in range(38):
                        if i + 2 < NCH:
                            sync.wait_ge(s_xr, i + 1)
                            sync.dma_start(Jr[:, ((i + 2) % 2) * D:((i + 2) % 2 + 1) * D],
                                           dJ[(i + 2) * 128:(i + 3) * 128, :]
                                           ).then_inc(s_jr[(i + 2) % 2], 16)
                        if 0 <= i - 6 < NCH:
                            ii = i - 6
                            sync.wait_ge(s_vq, ii + 1)
                            sync.dma_start(dattn[ii * 128:(ii + 1) * 128, :],
                                           qbuf[:, (ii % 2) * D:(ii % 2 + 1) * D]
                                           ).then_inc(s_aw[ii % 2], 16)
                    for g in range(2):
                        sync.wait_ge(s_rb, 8 + g + 1)
                        sync.dma_start(bass.AP(da2, g * 512, [[1, 1], [1, 512]]),
                                       rowb[:1, g * 512:(g + 1) * 512]).then_inc(s_a2w, 16)

                @blk.scalar
                def _(scalar):
                    for i in range(38):
                        if i < NCH:
                            scalar.wait_ge(s_jr[i % 2], 16 * (i // 2 + 1))
                            scalar.activation(xrb[:, (i % 2) * D:(i % 2 + 1) * D],
                                              Jr[:, (i % 2) * D:(i % 2 + 1) * D],
                                              ACT.Identity, bias=NMS[:, i:i + 1],
                                              scale=RS[:, i:i + 1]).then_inc(s_xr, 1)
                        if 0 <= i - 4 < NCH:
                            scalar.wait_ge(s_vrec2, i - 3)
                            scalar.sqrt(RS2[:, i - 4:i - 3],
                                        REC[:, i - 4:i - 3]).then_inc(s_rs2, 1)
                        if i in w2_slot:
                            t = w2_slot[i]
                            scalar.wait_ge(s_w2, 16 * 2 * (t + 1))
                            if t >= 1:
                                scalar.wait_ge(s_te2, t)
                            scalar.activation(absb[:], wTf[:], ACT.Abs).then_inc(s_ab2, 1)
                        if 0 <= i - 1 < NCH:
                            scalar.wait_ge(s_vc1, i)
                            scalar.activation(dumpA[:],
                                              tbuf[:, ((i - 1) % 6) * D:((i - 1) % 6 + 1) * D],
                                              ACT.Square,
                                              accum_out=SQ[:, i - 1:i]).then_inc(s_sq2, 1)
                        if 0 <= i - 5 < NCH:
                            ii = i - 5
                            scalar.wait_ge(s_vnm2, ii + 1)
                            scalar.activation(zbuf[:, (ii % 2) * D:(ii % 2 + 1) * D],
                                              tbuf[:, (ii % 6) * D:(ii % 6 + 1) * D],
                                              ACT.Identity, bias=NMS2[:, ii:ii + 1],
                                              scale=RS2[:, ii:ii + 1]).then_inc(s_z, 1)

                @blk.vector
                def _(vector):
                    for i in range(38):
                        if i < NCH:
                            vector.wait_ge(s_teC, i + 1)
                            vector.wait_ge(s_xr, i + 1)
                            vector.scalar_tensor_tensor(
                                tbuf[:, (i % 6) * D:(i % 6 + 1) * D],
                                pA[i % 2][:], ISQ,
                                xrb[:, (i % 2) * D:(i % 2 + 1) * D],
                                ALU.mult, ALU.add,
                                accum_out=SU[:, i:i + 1]).then_inc(s_vc1, 1)
                        if i in w2_slot:
                            t = w2_slot[i]
                            vector.wait_ge(s_w2, 16 * 2 * (t + 1))
                            vector.tensor_scalar(sgnb[:], wTf[:], 1e38, 1.0,
                                                 ALU.mult, ALU.min)
                        if 0 <= i - 2 < NCH:
                            vector.wait_ge(s_sq2, i - 1)
                            vector.tensor_scalar(E2E[:, i - 2:i - 1], SQ[:, i - 2:i - 1],
                                                 INV_D, EPS, ALU.mult, ALU.add)
                            vector.tensor_scalar(MU[:, i - 2:i - 1], SU[:, i - 2:i - 1],
                                                 INV_D, None, ALU.mult)
                            vector.tensor_scalar(NEGMU[:, i - 2:i - 1], SU[:, i - 2:i - 1],
                                                 -INV_D, None, ALU.mult)
                        if i in w2_slot:
                            t = w2_slot[i]
                            vector.tensor_scalar(S2T[:, 2 * t * D: 2 * t * D + 2048],
                                                 sgnb[:], -1.0, None,
                                                 ALU.max).then_inc(s_sg2, 1)
                        if 0 <= i - 3 < NCH:
                            vector.scalar_tensor_tensor(VAR[:, i - 3:i - 2],
                                                        MU[:, i - 3:i - 2],
                                                        NEGMU[:, i - 3:i - 2],
                                                        E2E[:, i - 3:i - 2],
                                                        ALU.mult, ALU.add)
                        if 0 <= i - 4 < NCH:
                            vector.reciprocal(REC[:, i - 4:i - 3],
                                              VAR[:, i - 4:i - 3]).then_inc(s_vrec2, 1)
                        if 0 <= i - 5 < NCH:
                            vector.wait_ge(s_rs2, i - 4)
                            vector.tensor_tensor(NMS2[:, i - 5:i - 4],
                                                 NEGMU[:, i - 5:i - 4],
                                                 RS2[:, i - 5:i - 4],
                                                 ALU.mult).then_inc(s_vnm2, 1)
                        if 0 <= i - 6 < NCH:
                            ii = i - 6
                            vector.wait_ge(s_z, ii + 1)
                            vector.wait_ge(s_gbc, 16)
                            if ii >= 2:
                                vector.wait_ge(s_aw[ii % 2], 16 * (ii // 2))
                            vector.tensor_tensor(qbuf[:, (ii % 2) * D:(ii % 2 + 1) * D],
                                                 zbuf[:, (ii % 2) * D:(ii % 2 + 1) * D],
                                                 gBC[:], ALU.mult).then_inc(s_vq, 1)
                    vector.wait_ge(s_te2, WG2)
                    for g in range(2):
                        vector.tensor_copy(rowb[:1, g * 512:(g + 1) * 512],
                                           pal2[g][:1, :]).then_inc(s_rb, 1)

                @blk.tensor
                def _(tensor):
                    tensor.wait_ge(s_mev, 1)
                    for i in range(NCH):
                        if i >= 2:
                            tensor.wait_ge(s_vc1, i - 1)
                        for hf in range(2):
                            mm = tensor.matmul(
                                pA[i % 2][:, hf * 512:(hf + 1) * 512],
                                QT[:, i * 128:(i + 1) * 128],
                                Msb[:, hf * 512:(hf + 1) * 512],
                                start=True, stop=True, skip_group_check=True)
                        mm.then_inc(s_teC, 1)
                        if i in w2_slot:
                            t = w2_slot[i]
                            tensor.wait_ge(s_ab2, t + 1)
                            for tt in range(2):
                                for g in range(2):
                                    mm2 = tensor.matmul(
                                        pal2[g][:1, :], onesH[:],
                                        absb[:, tt * D + g * 512: tt * D + g * 512 + 512],
                                        start=(t == 0 and tt == 0),
                                        stop=(t == WG2 - 1 and tt == 1),
                                        skip_group_check=True)
                            mm2.then_inc(s_te2, 1)

                @blk.gpsimd
                def _(gpsimd):
                    for t in range(WG2):
                        if t >= 1:
                            gpsimd.wait_ge(s_sg2, t)
                            gpsimd.wait_ge(s_ab2, t)
                        for tt in range(2):
                            ht = 2 * t + tt
                            gpsimd.dma_start(wTf[:, tt * D:(tt + 1) * D],
                                             dw2T[ht * 128:(ht + 1) * 128, :]
                                             ).then_inc(s_w2, 16)

        sAC.close()

        with ExitStack() as sD:
            XTD = sD.enter_context(nc.sbuf_tensor("XTD", [P, 2 * 2048], BF16))
            G = sD.enter_context(nc.sbuf_tensor("G", [P, HT * NB], BF16))
            a2BC = sD.enter_context(nc.sbuf_tensor("a2BC", [P, D], F32))
            CBC = sD.enter_context(nc.sbuf_tensor("CBC", [P, D], F32))
            a2row = sD.enter_context(nc.sbuf_tensor("a2row", [1, D], F32))
            atnr = sD.enter_context(nc.sbuf_tensor("atnr", [P, 2 * D], BF16))
            JrD = sD.enter_context(nc.sbuf_tensor("JrD", [P, 2 * D], F32))
            ubuf = sD.enter_context(nc.sbuf_tensor("ubuf", [P, 2 * D], F32))
            vbuf = sD.enter_context(nc.sbuf_tensor("vbuf", [P, D], F32))
            wbuf = sD.enter_context(nc.sbuf_tensor("wbuf", [P, 2 * D], F32))
            ostg = sD.enter_context(nc.sbuf_tensor("ostg", [P, 2 * D], F32))
            pg = [sD.enter_context(nc.psum_tensor(f"pg{x}", [P, NB], F32)) for x in range(2)]
            po = [sD.enter_context(nc.psum_tensor(f"po{x}", [P, D], F32)) for x in range(2)]
            with nc.Block() as blk:
                @blk.sync
                def _(sync):
                    sync.dma_start(ubuf[:, 0:D], dbB.ap()).then_inc(s_cbc, 16)
                    sync.dma_start(wbuf[:, 0:D], db2B.ap()).then_inc(s_cbc, 16)
                    sync.wait_ge(s_a2w, 32)
                    sync.dma_start(a2row[:], bass.AP(da2, 0, [[1, 1], [1, D]])
                                   ).then_inc(s_a2l, 16)
                    for i in range(2):
                        sync.wait_ge(s_aw[i % 2], 16 * (i // 2 + 1))
                        sync.dma_start(atnr[:, (i % 2) * D:(i % 2 + 1) * D],
                                       dattn[i * 128:(i + 1) * 128, :]).then_inc(s_apr[i % 2], 16)
                        sync.dma_start(JrD[:, (i % 2) * D:(i % 2 + 1) * D],
                                       dJ[i * 128:(i + 1) * 128, :]).then_inc(s_jr2[i % 2], 16)
                    for nbp in range(2):
                        for dc in range(DC):
                            sync.dma_start(
                                XTD[:, (nbp % 2) * 2048 + dc * 256:
                                       (nbp % 2) * 2048 + dc * 256 + 256],
                                bass.AP(dxnat, nbp * 256 * D + dc * 128,
                                        [[D, 256], [1, 128]]),
                                transpose=True).then_inc(s_trD[nbp % 2], 16)
                    for nb in range(NBLK):
                        if nb + 2 < NBLK:
                            nbp = nb + 2
                            sync.wait_ge(s_mm1, HT * (nbp - 1))
                            for dc in range(DC):
                                sync.dma_start(
                                    XTD[:, (nbp % 2) * 2048 + dc * 256:
                                           (nbp % 2) * 2048 + dc * 256 + 256],
                                    bass.AP(dxnat, nbp * 256 * D + dc * 128,
                                            [[D, 256], [1, 128]]),
                                    transpose=True).then_inc(s_trD[nbp % 2], 16)
                        for st in range(NSUB):
                            i = nb * NSUB + st
                            if i + 2 < NT:
                                sync.wait_ge(s_epv, i + 1)
                                sync.wait_ge(s_aw[(i + 2) % 2], 16 * ((i + 2) // 2 + 1))
                                sync.dma_start(atnr[:, ((i + 2) % 2) * D:((i + 2) % 2 + 1) * D],
                                               dattn[(i + 2) * 128:(i + 3) * 128, :]
                                               ).then_inc(s_apr[(i + 2) % 2], 16)
                                sync.wait_ge(s_vw, i + 1)
                                sync.dma_start(JrD[:, ((i + 2) % 2) * D:((i + 2) % 2 + 1) * D],
                                               dJ[(i + 2) * 128:(i + 3) * 128, :]
                                               ).then_inc(s_jr2[(i + 2) % 2], 16)
                            if i - 3 >= 0:
                                ii = i - 3
                                sync.wait_ge(s_vo, ii + 1)
                                sync.dma_start(dout[ii * 128:(ii + 1) * 128, :],
                                               ostg[:, (ii % 2) * D:(ii % 2 + 1) * D]
                                               ).then_inc(s_ow[ii % 2], 16)
                    for ii in range(NT - 3, NT):
                        sync.wait_ge(s_vo, ii + 1)
                        sync.dma_start(dout[ii * 128:(ii + 1) * 128, :],
                                       ostg[:, (ii % 2) * D:(ii % 2 + 1) * D]
                                       ).then_inc(s_ow[ii % 2], 16)
                    sync.wait_ge(s_ow[0], 16 * (NT // 2))
                    sync.wait_ge(s_ow[1], 16 * (NT // 2))

                @blk.vector
                def _(vector):
                    vector.wait_ge(s_abc, 1)
                    vector.tensor_copy(a2BC[:], po[0][:]).then_inc(s_abc, 1)
                    vector.wait_ge(s_cbc, 32)
                    vector.tensor_tensor(CBC[:], ubuf[:, 0:D], wbuf[:, 0:D],
                                         ALU.add).then_inc(s_cbc, 1)
                    for slot in range(NT + 3):
                        i = slot
                        if i < NT:
                            vector.wait_ge(s_mm2, i + 1)
                            vector.tensor_tensor(ubuf[:, (i % 2) * D:(i % 2 + 1) * D],
                                                 po[i % 2][:], a2BC[:],
                                                 ALU.mult).then_inc(s_epu, 1)
                        i2 = slot - 2
                        if 0 <= i2 < NT:
                            vector.wait_ge(s_jr2[i2 % 2], 16 * (i2 // 2 + 1))
                            vector.tensor_tensor(wbuf[:, (i2 % 2) * D:(i2 % 2 + 1) * D],
                                                 vbuf[:],
                                                 JrD[:, (i2 % 2) * D:(i2 % 2 + 1) * D],
                                                 ALU.add).then_inc(s_vw, 1)
                        i1 = slot - 1
                        if 0 <= i1 < NT:
                            vector.wait_ge(s_apr[i1 % 2], 16 * (i1 // 2 + 1))
                            vector.tensor_tensor(vbuf[:],
                                                 ubuf[:, (i1 % 2) * D:(i1 % 2 + 1) * D],
                                                 atnr[:, (i1 % 2) * D:(i1 % 2 + 1) * D],
                                                 ALU.add).then_inc(s_epv, 1)
                        i3 = slot - 3
                        if 0 <= i3 < NT:
                            if i3 >= 2:
                                vector.wait_ge(s_ow[i3 % 2], 16 * (i3 // 2))
                            vector.tensor_tensor(ostg[:, (i3 % 2) * D:(i3 % 2 + 1) * D],
                                                 wbuf[:, (i3 % 2) * D:(i3 % 2 + 1) * D],
                                                 CBC[:], ALU.add).then_inc(s_vo, 1)

                @blk.scalar
                def _(scalar):
                    scalar.wait_ge(s_a1l, 32)
                    for nb in range(NBLK):
                        for ht in range(HT):
                            g_idx = nb * HT + ht
                            scalar.wait_ge(s_mm1, g_idx + 1)
                            scalar.activation(G[:, ht * NB:(ht + 1) * NB],
                                              pg[ht % 2][:], ACT.Gelu,
                                              bias=b1P[:, ht:ht + 1],
                                              scale=alpha1P[:, ht:ht + 1]).then_inc(s_ge, 1)

                @blk.tensor
                def _(tensor):
                    tensor.wait_ge(s_a2l, 16)
                    for hf in range(2):
                        mm = tensor.matmul(po[0][:, hf * 512:(hf + 1) * 512],
                                           onesF[:], a2row[:1, hf * 512:(hf + 1) * 512],
                                           start=True, stop=True, skip_group_check=True)
                    mm.then_inc(s_abc, 1)
                    tensor.wait_ge(s_abc, 2)
                    tensor.wait_ge(s_sg1, WG1)
                    tensor.wait_ge(s_sg2, WG2)
                    for nb in range(NBLK):
                        xt0 = (nb % 2) * 2048
                        tensor.wait_ge(s_trD[nb % 2], 16 * 8 * (nb // 2 + 1))
                        for ht in range(HT):
                            g_idx = nb * HT + ht
                            if g_idx >= 2:
                                tensor.wait_ge(s_ge, g_idx - 1)
                            for dc in range(DC):
                                mm = tensor.matmul(
                                    pg[ht % 2][:],
                                    S1T[:, dc * H + ht * 128: dc * H + ht * 128 + 128],
                                    XTD[:, xt0 + dc * 256: xt0 + dc * 256 + 256],
                                    start=(dc == 0), stop=(dc == 7),
                                    skip_group_check=True)
                            mm.then_inc(s_mm1, 1)
                        tensor.wait_ge(s_ge, HT * (nb + 1))
                        for st in range(NSUB):
                            i = nb * NSUB + st
                            if i >= 2:
                                tensor.wait_ge(s_epu, i - 1)
                            for hc in range(HT):
                                for hf in range(2):
                                    mm = tensor.matmul(
                                        po[i % 2][:, hf * 512:(hf + 1) * 512],
                                        G[:, hc * NB + st * 128: hc * NB + st * 128 + 128],
                                        S2T[:, hc * D + hf * 512: hc * D + hf * 512 + 512],
                                        start=(hc == 0), stop=(hc == HT - 1),
                                        skip_group_check=True)
                            mm.then_inc(s_mm2, 1)

    return nc


_NC_CACHE = None


def kernel(L0_attn_J, L0_attn_W_Q, L0_attn_W_K, L0_attn_gamma, L0_attn_beta,
           L0_ffn_w1, L0_ffn_b1, L0_ffn_w2, L0_ffn_b2):
    J = np.ascontiguousarray(np.asarray(L0_attn_J, dtype=np.float32))
    wq = np.asarray(L0_attn_W_Q, dtype=np.float32)
    wk = np.asarray(L0_attn_W_K, dtype=np.float32)
    gamma = np.asarray(L0_attn_gamma, dtype=np.float32)
    beta = np.asarray(L0_attn_beta, dtype=np.float32)
    w1 = np.asarray(L0_ffn_w1, dtype=np.float32)
    b1 = np.asarray(L0_ffn_b1, dtype=np.float32)
    w2 = np.asarray(L0_ffn_w2, dtype=np.float32)
    b2 = np.asarray(L0_ffn_b2, dtype=np.float32)

    wqT = np.ascontiguousarray(wq.T)
    wkT = np.ascontiguousarray(wk.T)
    w1T = np.ascontiguousarray(w1.T)
    w2T = np.ascontiguousarray(w2.T)
    gB = np.ascontiguousarray(np.broadcast_to(gamma[None, :], (P, D)))
    bB = np.ascontiguousarray(np.broadcast_to(beta[None, :], (P, D)))
    b2B = np.ascontiguousarray(np.broadcast_to(b2[None, :], (P, D)))
    b1P = np.ascontiguousarray(b1.reshape(32, 128).T)

    global _NC_CACHE
    if _NC_CACHE is None:
        _NC_CACHE = build_nc()
    nc = _NC_CACHE

    in_maps = []
    for b in range(8):
        in_maps.append({
            "J": np.ascontiguousarray(J[b]),
            "wqT": wqT, "wkT": wkT, "w1T": w1T, "w2T": w2T,
            "gB": gB, "bB": bB, "b2B": b2B, "b1P": b1P,
        })
    res = run_bass_kernel_spmd(nc, in_maps, core_ids=list(range(8)))
    out = np.stack([np.asarray(r["out"], dtype=np.float32) for r in res.results], axis=0)
    return out


# version 02
# speedup vs baseline: 1.0732x; 1.0732x over previous
import math
import numpy as np
from contextlib import ExitStack

import concourse.bass as bass
import concourse.mybir as mybir
from concourse.bass_utils import run_bass_kernel_spmd

F32 = mybir.dt.float32
BF16 = mybir.dt.bfloat16
ALU = mybir.AluOpType
ACT = mybir.ActivationFunctionType
AX = mybir.AxisListType.X

P = 128
D = 1024
H = 4096
N = 4096
NCH = 32
DC = 8
HT = 32
NB = 256
NBLK = 16
NSUB = 2
NT = NBLK * NSUB
INV_D = 1.0 / 1024.0
EPS = 1e-5
ISQ = 1.0 / math.sqrt(128.0)
WG1 = 16
WG2 = 16


def build_nc():
    nc = bass.Bass("TRN2", target_bir_lowering=False, detect_race_conditions=False)

    dJ = nc.dram_tensor("J", [N, D], F32, kind="ExternalInput")
    dwqT = nc.dram_tensor("wqT", [D, 128], F32, kind="ExternalInput")
    dwkT = nc.dram_tensor("wkT", [D, 128], F32, kind="ExternalInput")
    dw1T = nc.dram_tensor("w1T", [D, H], F32, kind="ExternalInput")
    dw2T = nc.dram_tensor("w2T", [H, D], F32, kind="ExternalInput")
    dgB = nc.dram_tensor("gB", [P, D], F32, kind="ExternalInput")
    dbB = nc.dram_tensor("bB", [P, D], F32, kind="ExternalInput")
    db2B = nc.dram_tensor("b2B", [P, D], F32, kind="ExternalInput")
    db1P = nc.dram_tensor("b1P", [P, 32], F32, kind="ExternalInput")
    dout = nc.dram_tensor("out", [N, D], F32, kind="ExternalOutput")
    dxnat = nc.dram_tensor("xnat", [N, D], BF16, kind="Internal")
    dattn = nc.dram_tensor("attnscr", [N, D], BF16, kind="Internal")
    da1 = nc.dram_tensor("a1scr", [H], F32, kind="Internal")
    da2 = nc.dram_tensor("a2scr", [D], F32, kind="Internal")

    ctx = ExitStack()
    sb = lambda n, s, d: ctx.enter_context(nc.sbuf_tensor(n, s, d))
    sem = lambda n: ctx.enter_context(nc.semaphore(n))

    with ctx:
        S1T = sb("S1T", [P, DC * H], BF16)
        S2T = sb("S2T", [P, HT * D], BF16)
        alpha1P = sb("alpha1P", [P, 32], F32)
        b1P = sb("b1Ps", [P, 32], F32)
        onesA = sb("onesA", [P, 1], BF16)
        onesH = sb("onesH", [P, 1], BF16)
        onesF = sb("onesF", [1, P], F32)
        junk1 = sb("junk1", [P, 8], F32)
        jctr = [0]
        SU = sb("SU", [P, 32], F32)
        SQ = sb("SQ", [P, 32], F32)
        MU = sb("MU", [P, 32], F32)
        E2E = sb("E2E", [P, 32], F32)
        NEGMU = sb("NEGMU", [P, 32], F32)
        VAR = sb("VAR", [P, 32], F32)
        REC = sb("REC", [P, 32], F32)
        RS = sb("RS", [P, 32], F32)
        NMS = sb("NMS", [P, 32], F32)
        RS2 = sb("RS2", [P, 32], F32)
        NMS2 = sb("NMS2", [P, 32], F32)

        s_j = [sem(f"s_j{k}") for k in range(6)]
        s_xw = [sem(f"s_xw{k}") for k in range(3)]
        s_sq = sem("s_sq")
        s_vrec = sem("s_vrec"); s_rs = sem("s_rs"); s_vnm = sem("s_vnm")
        s_xn = sem("s_xn"); s_w1 = sem("s_w1"); s_sg1 = sem("s_sg1")
        s_ab1 = sem("s_ab1"); s_te1 = sem("s_te1"); s_rb = sem("s_rb")
        s_a1w = [sem(f"s_a1w{k}") for k in range(2)]
        s_a1l = sem("s_a1l")
        s_wqk = sem("s_wqk"); s_cast = sem("s_cast")
        s_trB = [sem(f"s_trB{k}") for k in range(2)]
        s_xc = [sem(f"s_xc{k}") for k in range(2)]
        s_teK = sem("s_teK")
        s_kev = sem("s_kev"); s_teMM = sem("s_teMM")
        s_teQ = sem("s_teQ"); s_qev = sem("s_qev"); s_mev = sem("s_mev")
        s_jr = [sem(f"s_jr{k}") for k in range(2)]
        s_xr = sem("s_xr"); s_vc1 = sem("s_vc1")
        s_sq2 = sem("s_sq2"); s_vrec2 = sem("s_vrec2"); s_rs2 = sem("s_rs2")
        s_vnm2 = sem("s_vnm2"); s_z = sem("s_z"); s_teC = sem("s_teC")
        s_vq = sem("s_vq")
        s_aw = [sem(f"s_aw{k}") for k in range(2)]
        s_w2 = sem("s_w2"); s_sg2 = sem("s_sg2"); s_ab2 = sem("s_ab2")
        s_te2 = sem("s_te2"); s_a2w = sem("s_a2w"); s_a2l = sem("s_a2l")
        s_gbc = sem("s_gbc"); s_cbc = sem("s_cbc"); s_abc = sem("s_abc")
        s_trD = [sem(f"s_trD{k}") for k in range(2)]
        s_mm1 = sem("s_mm1"); s_ge = sem("s_ge")
        s_mm2 = sem("s_mm2"); s_epu = sem("s_epu"); s_epv = sem("s_epv")
        s_vw = sem("s_vw"); s_vo = sem("s_vo")
        s_ow = [sem(f"s_ow{k}") for k in range(2)]
        s_apr = [sem(f"s_apr{k}") for k in range(2)]
        s_jr2 = [sem(f"s_jr2{k}") for k in range(2)]

        def w1_src(t):
            dc, hh = t // 2, t % 2
            return bass.AP(dw1T, dc * 128 * H + hh * 2048, [[H, 128], [1, 2048]])

        w1_slot = {2 + t: t for t in range(WG1)}
        w2_slot = {1 + 2 * t: t for t in range(WG2)}

        sAC = ExitStack()
        QT = sAC.enter_context(nc.sbuf_tensor("QT", [P, N], BF16))
        Msb = sAC.enter_context(nc.sbuf_tensor("Msb", [P, D], BF16))
        wTf = sAC.enter_context(nc.sbuf_tensor("wTf", [P, 2048], F32))
        sgnb = sAC.enter_context(nc.sbuf_tensor("sgnb", [P, 2048], BF16))
        absb = sAC.enter_context(nc.sbuf_tensor("absb", [P, 2048], BF16))
        rowb = sAC.enter_context(nc.sbuf_tensor("rowb", [1, 1024], F32))
        dumpA = sAC.enter_context(nc.sbuf_tensor("dumpA", [P, D], BF16))

        with ExitStack() as sA:
            pal = [sA.enter_context(nc.psum_tensor(f"pal{g}", [1, 512], F32))
                   for g in range(8)]
            Jring = sA.enter_context(nc.sbuf_tensor("Jring", [P, 6 * D], F32))
            XNr = sA.enter_context(nc.sbuf_tensor("XNr", [P, 3 * D], BF16))
            with nc.Block() as blk:
                @blk.sync
                def _(sync):
                    for c in range(4):
                        sync.dma_start(Jring[:, c * D:(c + 1) * D],
                                       dJ[c * 128:(c + 1) * 128, :]).then_inc(s_j[c % 6], 16)
                    for c in range(36):
                        if 0 <= c - 4 < NCH:
                            cc = c - 4
                            sync.wait_ge(s_xn, cc + 1)
                            sync.dma_start(dxnat[cc * 128:(cc + 1) * 128, :],
                                           XNr[:, (cc % 3) * D:(cc % 3 + 1) * D]
                                           ).then_inc(s_xw[cc % 3], 16)
                        if 4 <= c + 2 < NCH:
                            if c - 3 >= 1:
                                sync.wait_ge(s_xn, c - 3)
                            sync.dma_start(Jring[:, ((c + 2) % 6) * D:((c + 2) % 6 + 1) * D],
                                           dJ[(c + 2) * 128:(c + 3) * 128, :]
                                           ).then_inc(s_j[(c + 2) % 6], 16)
                    for g in range(8):
                        sync.wait_ge(s_rb, g + 1)
                        sync.dma_start(bass.AP(da1, g * 512, [[1, 1], [1, 512]]),
                                       rowb[:1, (g % 2) * 512:(g % 2) * 512 + 512]
                                       ).then_inc(s_a1w[g % 2], 16)
                    sync.wait_ge(s_a1w[0], 16 * 4)
                    sync.wait_ge(s_a1w[1], 16 * 4)
                    sync.dma_start(alpha1P[:], bass.AP(da1, 0, [[1, 128], [128, 32]]),
                                   allow_slow_non_contiguous=True).then_inc(s_a1l, 16)
                    sync.dma_start(b1P[:], db1P.ap()).then_inc(s_a1l, 16)

                @blk.vector
                def _(vector):
                    vector.memset(onesA[:], INV_D)
                    vector.memset(onesH[:], 1.0 / 4096.0)
                    vector.memset(onesF[:], 1.0)
                    for c in range(36):
                        if c < NCH:
                            vector.wait_ge(s_j[c % 6], 16 * (c // 6 + 1))
                            vector.reduce_sum(SU[:, c:c + 1],
                                              Jring[:, (c % 6) * D:(c % 6 + 1) * D],
                                              axis=AX)
                        if c in w1_slot:
                            t = w1_slot[c]
                            vector.wait_ge(s_w1, 16 * (t + 1))
                            vector.tensor_scalar(sgnb[:], wTf[:], 1e38, 1.0,
                                                 ALU.mult, ALU.min)
                        if 0 <= c - 1 < NCH:
                            vector.wait_ge(s_sq, c)
                            vector.tensor_scalar(E2E[:, c - 1:c], SQ[:, c - 1:c],
                                                 INV_D, EPS, ALU.mult, ALU.add)
                            vector.tensor_scalar(MU[:, c - 1:c], SU[:, c - 1:c],
                                                 INV_D, None, ALU.mult)
                            vector.tensor_scalar(NEGMU[:, c - 1:c], SU[:, c - 1:c],
                                                 -INV_D, None, ALU.mult)
                        if c in w1_slot:
                            t = w1_slot[c]
                            dc, hh = t // 2, t % 2
                            vector.tensor_scalar(
                                S1T[:, dc * H + hh * 2048: dc * H + hh * 2048 + 2048],
                                sgnb[:], -1.0, None, ALU.max).then_inc(s_sg1, 1)
                        if 0 <= c - 2 < NCH:
                            vector.scalar_tensor_tensor(VAR[:, c - 2:c - 1],
                                                        MU[:, c - 2:c - 1],
                                                        NEGMU[:, c - 2:c - 1],
                                                        E2E[:, c - 2:c - 1],
                                                        ALU.mult, ALU.add)
                        if 0 <= c - 3 < NCH:
                            vector.reciprocal(REC[:, c - 3:c - 2],
                                              VAR[:, c - 3:c - 2]).then_inc(s_vrec, 1)
                        if 0 <= c - 4 < NCH:
                            vector.wait_ge(s_rs, c - 3)
                            vector.tensor_tensor(NMS[:, c - 4:c - 3],
                                                 NEGMU[:, c - 4:c - 3],
                                                 RS[:, c - 4:c - 3],
                                                 ALU.mult).then_inc(s_vnm, 1)
                    vector.wait_ge(s_te1, WG1)
                    for g in range(8):
                        if g >= 2:
                            vector.wait_ge(s_a1w[g % 2], 16 * (g // 2))
                        vector.tensor_copy(rowb[:1, (g % 2) * 512:(g % 2) * 512 + 512],
                                           pal[g][:1, :]).then_inc(s_rb, 1)

                @blk.scalar
                def _(scalar):
                    for c in range(36):
                        if 0 <= c - 3 < NCH:
                            scalar.wait_ge(s_vrec, c - 2)
                            scalar.sqrt(RS[:, c - 3:c - 2],
                                        REC[:, c - 3:c - 2]).then_inc(s_rs, 1)
                        if c < NCH:
                            scalar.wait_ge(s_j[c % 6], 16 * (c // 6 + 1))
                            scalar.activation(dumpA[:],
                                              Jring[:, (c % 6) * D:(c % 6 + 1) * D],
                                              ACT.Square,
                                              accum_out=SQ[:, c:c + 1]).then_inc(s_sq, 1)
                        if c in w1_slot:
                            t = w1_slot[c]
                            scalar.wait_ge(s_w1, 16 * (t + 1))
                            if t >= 1:
                                scalar.wait_ge(s_te1, t)
                            scalar.activation(absb[:], wTf[:], ACT.Abs).then_inc(s_ab1, 1)
                        if 0 <= c - 4 < NCH:
                            cc = c - 4
                            scalar.wait_ge(s_vnm, cc + 1)
                            if cc >= 3:
                                scalar.wait_ge(s_xw[cc % 3], 16 * (cc // 3))
                            scalar.activation(XNr[:, (cc % 3) * D:(cc % 3 + 1) * D],
                                              Jring[:, (cc % 6) * D:(cc % 6 + 1) * D],
                                              ACT.Identity, bias=NMS[:, cc:cc + 1],
                                              scale=RS[:, cc:cc + 1]).then_inc(s_xn, 1)

                @blk.gpsimd
                def _(gpsimd):
                    for t in range(WG1):
                        if t >= 1:
                            gpsimd.wait_ge(s_sg1, t)
                            gpsimd.wait_ge(s_ab1, t)
                        gpsimd.dma_start(wTf[:], w1_src(t)).then_inc(s_w1, 16)

                @blk.tensor
                def _(tensor):
                    for t in range(WG1):
                        dc, hh = t // 2, t % 2
                        tensor.wait_ge(s_ab1, t + 1)
                        for g in range(4):
                            mm = tensor.matmul(pal[4 * hh + g][:1, :], onesA[:],
                                               absb[:, g * 512:(g + 1) * 512],
                                               start=(dc == 0), stop=(dc == 7),
                                               skip_group_check=True)
                        mm.then_inc(s_te1, 1)

        with ExitStack() as sB:
            XT = sB.enter_context(nc.sbuf_tensor("XT", [P, 2 * 8192], BF16))
            XNc = sB.enter_context(nc.sbuf_tensor("XNc", [P, 2 * D], BF16))
            Kb = sB.enter_context(nc.sbuf_tensor("Kb", [P, 2 * 128], BF16))
            WQTb = sB.enter_context(nc.sbuf_tensor("WQTb", [P, D], BF16))
            WKTb = sB.enter_context(nc.sbuf_tensor("WKTb", [P, D], BF16))
            pK = [sB.enter_context(nc.psum_tensor(f"pK{x}", [P, 128], F32)) for x in range(2)]
            pQT = [sB.enter_context(nc.psum_tensor(f"pQT{x}", [P, 256], F32)) for x in range(2)]
            pM = sB.enter_context(nc.psum_tensor("pM", [P, D], F32))
            with nc.Block() as blk:
                @blk.sync
                def _(sync):
                    for q in range(4):
                        if q >= 2:
                            sync.wait_ge(s_teQ, 4 * (q - 2) + 4)
                        for k in range(3):
                            ndone = 8 * (q + 1)
                            cnt = (ndone - 1 - k) // 3 + 1 if ndone - 1 >= k else 0
                            if cnt > 0:
                                sync.wait_ge(s_xw[k], 16 * cnt)
                        for dc in range(DC):
                            sync.dma_start(
                                XT[:, (q % 2) * 8192 + dc * 1024:
                                      (q % 2) * 8192 + dc * 1024 + 1024],
                                bass.AP(dxnat, q * 1024 * D + dc * 128,
                                        [[D, 1024], [1, 128]]),
                                transpose=True).then_inc(s_trB[q % 2], 16)

                @blk.gpsimd
                def _(gpsimd):
                    for dc in range(DC):
                        gpsimd.dma_start(WQTb[:, dc * 128:(dc + 1) * 128],
                                         dwqT[dc * 128:(dc + 1) * 128, :]).then_inc(s_wqk, 16)
                    for dc in range(DC):
                        gpsimd.dma_start(WKTb[:, dc * 128:(dc + 1) * 128],
                                         dwkT[dc * 128:(dc + 1) * 128, :]).then_inc(s_wqk, 16)

                @blk.scalar
                def _(scalar):
                    for c in range(NCH):
                        if c >= 2:
                            scalar.wait_ge(s_teMM, c - 1)
                        scalar.dma_start(XNc[:, (c % 2) * D:(c % 2 + 1) * D],
                                         dxnat[c * 128:(c + 1) * 128, :]).then_inc(s_xc[c % 2], 16)

                @blk.vector
                def _(vector):
                    for nb in range(NBLK):
                        for j in range(2):
                            c = 2 * nb + j
                            vector.wait_ge(s_teK, c + 1)
                            vector.tensor_copy(Kb[:, (c % 2) * 128:(c % 2 + 1) * 128],
                                               pK[c % 2][:]).then_inc(s_kev, 1)
                        vector.wait_ge(s_teQ, nb + 1)
                        vector.tensor_copy(QT[:, nb * 256:(nb + 1) * 256],
                                           pQT[nb % 2][:]).then_inc(s_qev, 1)
                    vector.wait_ge(s_teMM, NCH)
                    vector.tensor_copy(Msb[:], pM[:]).then_inc(s_mev, 1)

                @blk.tensor
                def _(tensor):
                    tensor.wait_ge(s_wqk, 16 * 16)
                    for nb in range(NBLK):
                        q = nb // 4
                        xt0 = (q % 2) * 8192
                        off = (nb % 4) * 256
                        tensor.wait_ge(s_trB[q % 2], 16 * 8 * (q // 2 + 1))
                        for j in range(2):
                            c = 2 * nb + j
                            if c >= 2:
                                tensor.wait_ge(s_kev, c - 1)
                            for dc in range(DC):
                                mm = tensor.matmul(
                                    pK[c % 2][:],
                                    XT[:, xt0 + dc * 1024 + off + j * 128:
                                          xt0 + dc * 1024 + off + j * 128 + 128],
                                    WKTb[:, dc * 128:(dc + 1) * 128],
                                    start=(dc == 0), stop=(dc == 7),
                                    skip_group_check=True)
                            mm.then_inc(s_teK, 1)
                        for j in range(2):
                            c = 2 * nb + j
                            tensor.wait_ge(s_kev, c + 1)
                            tensor.wait_ge(s_xc[c % 2], 16 * (c // 2 + 1))
                            for hf in range(2):
                                mm = tensor.matmul(
                                    pM[:, hf * 512:(hf + 1) * 512],
                                    Kb[:, (c % 2) * 128:(c % 2 + 1) * 128],
                                    XNc[:, (c % 2) * D + hf * 512:
                                           (c % 2) * D + hf * 512 + 512],
                                    start=(c == 0), stop=(c == NCH - 1),
                                    skip_group_check=True)
                            mm.then_inc(s_teMM, 1)
                        if nb >= 2:
                            tensor.wait_ge(s_qev, nb - 1)
                        for dc in range(DC):
                            mm = tensor.matmul(
                                pQT[nb % 2][:],
                                WQTb[:, dc * 128:(dc + 1) * 128],
                                XT[:, xt0 + dc * 1024 + off: xt0 + dc * 1024 + off + 256],
                                start=(dc == 0), stop=(dc == 7),
                                skip_group_check=True)
                        mm.then_inc(s_teQ, 1)

        with ExitStack() as sC:
            Jr = sC.enter_context(nc.sbuf_tensor("Jr", [P, 2 * D], F32))
            xrb = sC.enter_context(nc.sbuf_tensor("xrb", [P, 2 * D], BF16))
            tbuf = sC.enter_context(nc.sbuf_tensor("tbuf", [P, 6 * D], BF16))
            zbuf = sC.enter_context(nc.sbuf_tensor("zbuf", [P, 2 * D], BF16))
            qbuf = sC.enter_context(nc.sbuf_tensor("qbuf", [P, 2 * D], BF16))
            gBC = sC.enter_context(nc.sbuf_tensor("gBC", [P, D], F32))
            pA = [sC.enter_context(nc.psum_tensor(f"pA{x}", [P, D], F32)) for x in range(2)]
            pal2 = [sC.enter_context(nc.psum_tensor(f"pal2{x}", [1, 512], F32))
                    for x in range(2)]
            with nc.Block() as blk:
                @blk.sync
                def _(sync):
                    sync.dma_start(gBC[:], dgB.ap()).then_inc(s_gbc, 16)
                    for i in range(2):
                        sync.dma_start(Jr[:, (i % 2) * D:(i % 2 + 1) * D],
                                       dJ[i * 128:(i + 1) * 128, :]).then_inc(s_jr[i % 2], 16)
                    for i in range(38):
                        if i + 2 < NCH:
                            sync.wait_ge(s_xr, i + 1)
                            sync.dma_start(Jr[:, ((i + 2) % 2) * D:((i + 2) % 2 + 1) * D],
                                           dJ[(i + 2) * 128:(i + 3) * 128, :]
                                           ).then_inc(s_jr[(i + 2) % 2], 16)
                        if 0 <= i - 6 < NCH:
                            ii = i - 6
                            sync.wait_ge(s_vq, ii + 1)
                            sync.dma_start(dattn[ii * 128:(ii + 1) * 128, :],
                                           qbuf[:, (ii % 2) * D:(ii % 2 + 1) * D]
                                           ).then_inc(s_aw[ii % 2], 16)
                    for g in range(2):
                        sync.wait_ge(s_rb, 8 + g + 1)
                        sync.dma_start(bass.AP(da2, g * 512, [[1, 1], [1, 512]]),
                                       rowb[:1, g * 512:(g + 1) * 512]).then_inc(s_a2w, 16)

                @blk.scalar
                def _(scalar):
                    for i in range(38):
                        if i < NCH:
                            scalar.wait_ge(s_jr[i % 2], 16 * (i // 2 + 1))
                            scalar.activation(xrb[:, (i % 2) * D:(i % 2 + 1) * D],
                                              Jr[:, (i % 2) * D:(i % 2 + 1) * D],
                                              ACT.Identity, bias=NMS[:, i:i + 1],
                                              scale=RS[:, i:i + 1]).then_inc(s_xr, 1)
                        if 0 <= i - 4 < NCH:
                            scalar.wait_ge(s_vrec2, i - 3)
                            scalar.sqrt(RS2[:, i - 4:i - 3],
                                        REC[:, i - 4:i - 3]).then_inc(s_rs2, 1)
                        if i in w2_slot:
                            t = w2_slot[i]
                            scalar.wait_ge(s_w2, 16 * 2 * (t + 1))
                            if t >= 1:
                                scalar.wait_ge(s_te2, t)
                            scalar.activation(absb[:], wTf[:], ACT.Abs).then_inc(s_ab2, 1)
                        if 0 <= i - 1 < NCH:
                            scalar.wait_ge(s_vc1, i)
                            scalar.activation(dumpA[:],
                                              tbuf[:, ((i - 1) % 6) * D:((i - 1) % 6 + 1) * D],
                                              ACT.Square,
                                              accum_out=SQ[:, i - 1:i]).then_inc(s_sq2, 1)
                        if 0 <= i - 5 < NCH:
                            ii = i - 5
                            scalar.wait_ge(s_vnm2, ii + 1)
                            scalar.activation(zbuf[:, (ii % 2) * D:(ii % 2 + 1) * D],
                                              tbuf[:, (ii % 6) * D:(ii % 6 + 1) * D],
                                              ACT.Identity, bias=NMS2[:, ii:ii + 1],
                                              scale=RS2[:, ii:ii + 1]).then_inc(s_z, 1)

                @blk.vector
                def _(vector):
                    for i in range(38):
                        if i < NCH:
                            vector.wait_ge(s_teC, i + 1)
                            vector.wait_ge(s_xr, i + 1)
                            vector.scalar_tensor_tensor(
                                tbuf[:, (i % 6) * D:(i % 6 + 1) * D],
                                pA[i % 2][:], ISQ,
                                xrb[:, (i % 2) * D:(i % 2 + 1) * D],
                                ALU.mult, ALU.add,
                                accum_out=SU[:, i:i + 1]).then_inc(s_vc1, 1)
                        if i in w2_slot:
                            t = w2_slot[i]
                            vector.wait_ge(s_w2, 16 * 2 * (t + 1))
                            vector.tensor_scalar(sgnb[:], wTf[:], 1e38, 1.0,
                                                 ALU.mult, ALU.min)
                        if 0 <= i - 2 < NCH:
                            vector.wait_ge(s_sq2, i - 1)
                            vector.tensor_scalar(E2E[:, i - 2:i - 1], SQ[:, i - 2:i - 1],
                                                 INV_D, EPS, ALU.mult, ALU.add)
                            vector.tensor_scalar(MU[:, i - 2:i - 1], SU[:, i - 2:i - 1],
                                                 INV_D, None, ALU.mult)
                            vector.tensor_scalar(NEGMU[:, i - 2:i - 1], SU[:, i - 2:i - 1],
                                                 -INV_D, None, ALU.mult)
                        if i in w2_slot:
                            t = w2_slot[i]
                            vector.tensor_scalar(S2T[:, 2 * t * D: 2 * t * D + 2048],
                                                 sgnb[:], -1.0, None,
                                                 ALU.max).then_inc(s_sg2, 1)
                        if 0 <= i - 3 < NCH:
                            vector.scalar_tensor_tensor(VAR[:, i - 3:i - 2],
                                                        MU[:, i - 3:i - 2],
                                                        NEGMU[:, i - 3:i - 2],
                                                        E2E[:, i - 3:i - 2],
                                                        ALU.mult, ALU.add)
                        if 0 <= i - 4 < NCH:
                            vector.reciprocal(REC[:, i - 4:i - 3],
                                              VAR[:, i - 4:i - 3]).then_inc(s_vrec2, 1)
                        if 0 <= i - 5 < NCH:
                            vector.wait_ge(s_rs2, i - 4)
                            vector.tensor_tensor(NMS2[:, i - 5:i - 4],
                                                 NEGMU[:, i - 5:i - 4],
                                                 RS2[:, i - 5:i - 4],
                                                 ALU.mult).then_inc(s_vnm2, 1)
                        if 0 <= i - 6 < NCH:
                            ii = i - 6
                            vector.wait_ge(s_z, ii + 1)
                            vector.wait_ge(s_gbc, 16)
                            if ii >= 2:
                                vector.wait_ge(s_aw[ii % 2], 16 * (ii // 2))
                            vector.tensor_tensor(qbuf[:, (ii % 2) * D:(ii % 2 + 1) * D],
                                                 zbuf[:, (ii % 2) * D:(ii % 2 + 1) * D],
                                                 gBC[:], ALU.mult).then_inc(s_vq, 1)
                    vector.wait_ge(s_te2, WG2)
                    for g in range(2):
                        vector.tensor_copy(rowb[:1, g * 512:(g + 1) * 512],
                                           pal2[g][:1, :]).then_inc(s_rb, 1)

                @blk.tensor
                def _(tensor):
                    tensor.wait_ge(s_mev, 1)
                    for i in range(NCH):
                        if i >= 2:
                            tensor.wait_ge(s_vc1, i - 1)
                        for hf in range(2):
                            mm = tensor.matmul(
                                pA[i % 2][:, hf * 512:(hf + 1) * 512],
                                QT[:, i * 128:(i + 1) * 128],
                                Msb[:, hf * 512:(hf + 1) * 512],
                                start=True, stop=True, skip_group_check=True)
                        mm.then_inc(s_teC, 1)
                        if i in w2_slot:
                            t = w2_slot[i]
                            tensor.wait_ge(s_ab2, t + 1)
                            for tt in range(2):
                                for g in range(2):
                                    mm2 = tensor.matmul(
                                        pal2[g][:1, :], onesH[:],
                                        absb[:, tt * D + g * 512: tt * D + g * 512 + 512],
                                        start=(t == 0 and tt == 0),
                                        stop=(t == WG2 - 1 and tt == 1),
                                        skip_group_check=True)
                            mm2.then_inc(s_te2, 1)

                @blk.gpsimd
                def _(gpsimd):
                    for t in range(WG2):
                        if t >= 1:
                            gpsimd.wait_ge(s_sg2, t)
                            gpsimd.wait_ge(s_ab2, t)
                        for tt in range(2):
                            ht = 2 * t + tt
                            gpsimd.dma_start(wTf[:, tt * D:(tt + 1) * D],
                                             dw2T[ht * 128:(ht + 1) * 128, :]
                                             ).then_inc(s_w2, 16)

        sAC.close()

        with ExitStack() as sD:
            XTD = sD.enter_context(nc.sbuf_tensor("XTD", [P, 2 * 2048], BF16))
            G = sD.enter_context(nc.sbuf_tensor("G", [P, HT * NB], BF16))
            a2BC = sD.enter_context(nc.sbuf_tensor("a2BC", [P, D], F32))
            CBC = sD.enter_context(nc.sbuf_tensor("CBC", [P, D], F32))
            a2row = sD.enter_context(nc.sbuf_tensor("a2row", [1, D], F32))
            atnr = sD.enter_context(nc.sbuf_tensor("atnr", [P, 2 * D], BF16))
            JrD = sD.enter_context(nc.sbuf_tensor("JrD", [P, 2 * D], F32))
            ubuf = sD.enter_context(nc.sbuf_tensor("ubuf", [P, 2 * D], F32))
            vbuf = sD.enter_context(nc.sbuf_tensor("vbuf", [P, D], F32))
            wbuf = sD.enter_context(nc.sbuf_tensor("wbuf", [P, 2 * D], F32))
            ostg = sD.enter_context(nc.sbuf_tensor("ostg", [P, 2 * D], F32))
            pg = [sD.enter_context(nc.psum_tensor(f"pg{x}", [P, NB], F32)) for x in range(4)]
            po = [sD.enter_context(nc.psum_tensor(f"po{x}", [P, D], F32)) for x in range(2)]
            with nc.Block() as blk:
                @blk.sync
                def _(sync):
                    sync.dma_start(ubuf[:, 0:D], dbB.ap()).then_inc(s_cbc, 16)
                    sync.dma_start(wbuf[:, 0:D], db2B.ap()).then_inc(s_cbc, 16)
                    sync.wait_ge(s_a2w, 32)
                    sync.dma_start(a2row[:], bass.AP(da2, 0, [[1, 1], [1, D]])
                                   ).then_inc(s_a2l, 16)
                    for i in range(2):
                        sync.wait_ge(s_aw[i % 2], 16 * (i // 2 + 1))
                        sync.dma_start(atnr[:, (i % 2) * D:(i % 2 + 1) * D],
                                       dattn[i * 128:(i + 1) * 128, :]).then_inc(s_apr[i % 2], 16)
                        sync.dma_start(JrD[:, (i % 2) * D:(i % 2 + 1) * D],
                                       dJ[i * 128:(i + 1) * 128, :]).then_inc(s_jr2[i % 2], 16)
                    for nbp in range(2):
                        for dc in range(DC):
                            sync.dma_start(
                                XTD[:, (nbp % 2) * 2048 + dc * 256:
                                       (nbp % 2) * 2048 + dc * 256 + 256],
                                bass.AP(dxnat, nbp * 256 * D + dc * 128,
                                        [[D, 256], [1, 128]]),
                                transpose=True).then_inc(s_trD[nbp % 2], 16)
                    for nb in range(NBLK):
                        if nb + 2 < NBLK:
                            nbp = nb + 2
                            sync.wait_ge(s_mm1, HT * (nbp - 1))
                            for dc in range(DC):
                                sync.dma_start(
                                    XTD[:, (nbp % 2) * 2048 + dc * 256:
                                           (nbp % 2) * 2048 + dc * 256 + 256],
                                    bass.AP(dxnat, nbp * 256 * D + dc * 128,
                                            [[D, 256], [1, 128]]),
                                    transpose=True).then_inc(s_trD[nbp % 2], 16)
                        for st in range(NSUB):
                            i = nb * NSUB + st
                            if i + 2 < NT:
                                sync.wait_ge(s_epv, i + 1)
                                sync.wait_ge(s_aw[(i + 2) % 2], 16 * ((i + 2) // 2 + 1))
                                sync.dma_start(atnr[:, ((i + 2) % 2) * D:((i + 2) % 2 + 1) * D],
                                               dattn[(i + 2) * 128:(i + 3) * 128, :]
                                               ).then_inc(s_apr[(i + 2) % 2], 16)
                                sync.wait_ge(s_vw, i + 1)
                                sync.dma_start(JrD[:, ((i + 2) % 2) * D:((i + 2) % 2 + 1) * D],
                                               dJ[(i + 2) * 128:(i + 3) * 128, :]
                                               ).then_inc(s_jr2[(i + 2) % 2], 16)
                            if i - 3 >= 0:
                                ii = i - 3
                                sync.wait_ge(s_vo, ii + 1)
                                sync.dma_start(dout[ii * 128:(ii + 1) * 128, :],
                                               ostg[:, (ii % 2) * D:(ii % 2 + 1) * D]
                                               ).then_inc(s_ow[ii % 2], 16)
                    for ii in range(NT - 3, NT):
                        sync.wait_ge(s_vo, ii + 1)
                        sync.dma_start(dout[ii * 128:(ii + 1) * 128, :],
                                       ostg[:, (ii % 2) * D:(ii % 2 + 1) * D]
                                       ).then_inc(s_ow[ii % 2], 16)
                    sync.wait_ge(s_ow[0], 16 * (NT // 2))
                    sync.wait_ge(s_ow[1], 16 * (NT // 2))

                @blk.vector
                def _(vector):
                    vector.wait_ge(s_abc, 1)
                    vector.tensor_copy(a2BC[:], po[0][:]).then_inc(s_abc, 1)
                    vector.wait_ge(s_cbc, 32)
                    vector.tensor_tensor(CBC[:], ubuf[:, 0:D], wbuf[:, 0:D],
                                         ALU.add).then_inc(s_cbc, 1)
                    for slot in range(NT + 3):
                        i = slot
                        if i < NT:
                            vector.wait_ge(s_mm2, i + 1)
                            vector.tensor_tensor(ubuf[:, (i % 2) * D:(i % 2 + 1) * D],
                                                 po[i % 2][:], a2BC[:],
                                                 ALU.mult).then_inc(s_epu, 1)
                        i2 = slot - 2
                        if 0 <= i2 < NT:
                            vector.wait_ge(s_jr2[i2 % 2], 16 * (i2 // 2 + 1))
                            vector.tensor_tensor(wbuf[:, (i2 % 2) * D:(i2 % 2 + 1) * D],
                                                 vbuf[:],
                                                 JrD[:, (i2 % 2) * D:(i2 % 2 + 1) * D],
                                                 ALU.add).then_inc(s_vw, 1)
                        i1 = slot - 1
                        if 0 <= i1 < NT:
                            vector.wait_ge(s_apr[i1 % 2], 16 * (i1 // 2 + 1))
                            vector.tensor_tensor(vbuf[:],
                                                 ubuf[:, (i1 % 2) * D:(i1 % 2 + 1) * D],
                                                 atnr[:, (i1 % 2) * D:(i1 % 2 + 1) * D],
                                                 ALU.add).then_inc(s_epv, 1)
                        i3 = slot - 3
                        if 0 <= i3 < NT:
                            if i3 >= 2:
                                vector.wait_ge(s_ow[i3 % 2], 16 * (i3 // 2))
                            vector.tensor_tensor(ostg[:, (i3 % 2) * D:(i3 % 2 + 1) * D],
                                                 wbuf[:, (i3 % 2) * D:(i3 % 2 + 1) * D],
                                                 CBC[:], ALU.add).then_inc(s_vo, 1)

                @blk.scalar
                def _(scalar):
                    scalar.wait_ge(s_a1l, 32)
                    for nb in range(NBLK):
                        for ht in range(HT):
                            g_idx = nb * HT + ht
                            scalar.wait_ge(s_mm1, g_idx + 1)
                            scalar.activation(G[:, ht * NB:(ht + 1) * NB],
                                              pg[g_idx % 4][:], ACT.Gelu,
                                              bias=b1P[:, ht:ht + 1],
                                              scale=alpha1P[:, ht:ht + 1]).then_inc(s_ge, 1)

                @blk.tensor
                def _(tensor):
                    tensor.wait_ge(s_a2l, 16)
                    for hf in range(2):
                        mm = tensor.matmul(po[0][:, hf * 512:(hf + 1) * 512],
                                           onesF[:], a2row[:1, hf * 512:(hf + 1) * 512],
                                           start=True, stop=True, skip_group_check=True)
                    mm.then_inc(s_abc, 1)
                    tensor.wait_ge(s_abc, 2)
                    tensor.wait_ge(s_sg1, WG1)
                    tensor.wait_ge(s_sg2, WG2)
                    for nb in range(NBLK):
                        xt0 = (nb % 2) * 2048
                        tensor.wait_ge(s_trD[nb % 2], 16 * 8 * (nb // 2 + 1))
                        for ht in range(HT):
                            g_idx = nb * HT + ht
                            if g_idx >= 4:
                                tensor.wait_ge(s_ge, g_idx - 3)
                            for dc in range(DC):
                                mm = tensor.matmul(
                                    pg[g_idx % 4][:],
                                    S1T[:, dc * H + ht * 128: dc * H + ht * 128 + 128],
                                    XTD[:, xt0 + dc * 256: xt0 + dc * 256 + 256],
                                    start=(dc == 0), stop=(dc == 7),
                                    skip_group_check=True)
                            mm.then_inc(s_mm1, 1)
                        tensor.wait_ge(s_ge, HT * (nb + 1))
                        for st in range(NSUB):
                            i = nb * NSUB + st
                            if i >= 2:
                                tensor.wait_ge(s_epu, i - 1)
                            for hc in range(HT):
                                for hf in range(2):
                                    mm = tensor.matmul(
                                        po[i % 2][:, hf * 512:(hf + 1) * 512],
                                        G[:, hc * NB + st * 128: hc * NB + st * 128 + 128],
                                        S2T[:, hc * D + hf * 512: hc * D + hf * 512 + 512],
                                        start=(hc == 0), stop=(hc == HT - 1),
                                        skip_group_check=True)
                            mm.then_inc(s_mm2, 1)

    return nc


_NC_CACHE = None


def kernel(L0_attn_J, L0_attn_W_Q, L0_attn_W_K, L0_attn_gamma, L0_attn_beta,
           L0_ffn_w1, L0_ffn_b1, L0_ffn_w2, L0_ffn_b2):
    J = np.ascontiguousarray(np.asarray(L0_attn_J, dtype=np.float32))
    wq = np.asarray(L0_attn_W_Q, dtype=np.float32)
    wk = np.asarray(L0_attn_W_K, dtype=np.float32)
    gamma = np.asarray(L0_attn_gamma, dtype=np.float32)
    beta = np.asarray(L0_attn_beta, dtype=np.float32)
    w1 = np.asarray(L0_ffn_w1, dtype=np.float32)
    b1 = np.asarray(L0_ffn_b1, dtype=np.float32)
    w2 = np.asarray(L0_ffn_w2, dtype=np.float32)
    b2 = np.asarray(L0_ffn_b2, dtype=np.float32)

    wqT = np.ascontiguousarray(wq.T)
    wkT = np.ascontiguousarray(wk.T)
    w1T = np.ascontiguousarray(w1.T)
    w2T = np.ascontiguousarray(w2.T)
    gB = np.ascontiguousarray(np.broadcast_to(gamma[None, :], (P, D)))
    bB = np.ascontiguousarray(np.broadcast_to(beta[None, :], (P, D)))
    b2B = np.ascontiguousarray(np.broadcast_to(b2[None, :], (P, D)))
    b1P = np.ascontiguousarray(b1.reshape(32, 128).T)

    global _NC_CACHE
    if _NC_CACHE is None:
        _NC_CACHE = build_nc()
    nc = _NC_CACHE

    in_maps = []
    for b in range(8):
        in_maps.append({
            "J": np.ascontiguousarray(J[b]),
            "wqT": wqT, "wkT": wkT, "w1T": w1T, "w2T": w2T,
            "gB": gB, "bB": bB, "b2B": b2B, "b1P": b1P,
        })
    res = run_bass_kernel_spmd(nc, in_maps, core_ids=list(range(8)))
    out = np.stack([np.asarray(r["out"], dtype=np.float32) for r in res.results], axis=0)
    return out
